# version 02
# baseline (speedup 1.0000x reference)
import numpy as np
import ml_dtypes
import concourse.bass as bass
import concourse.mybir as mybir
from concourse.bass_utils import run_bass_kernel_spmd

F32 = mybir.dt.float32
BF16 = mybir.dt.bfloat16
AF = mybir.ActivationFunctionType
ALU = mybir.AluOpType
AX = mybir.AxisListType

H = 4096
CH = 32
TOK = 1024
TT = 512
NTB = 2
KVH = 8
QH = 32
IT = 86
EPS_LN = 1.0e-5
EPS_ATT = 1.0e-10
CCW = KVH * 128 + KVH

ENGS = ("sync", "gpsimd", "tensor", "scalar", "vector")


class Prog:
    def __init__(self):
        self.q = {e: [] for e in ENGS}
        self.hi = {e: {} for e in ENGS}

    def add(self, eng, fn):
        self.q[eng].append(fn)

    def wait(self, eng, sem, val):
        if val <= 0:
            return
        key = id(sem)
        if self.hi[eng].get(key, -1) >= val:
            return
        self.hi[eng][key] = val
        self.q[eng].append(lambda e, s=sem, v=val: e.wait_ge(s, v))

    def emit(self, block):
        for name in ENGS:
            ops = self.q[name]
            if not ops:
                continue
            getattr(block, name)(lambda e, ops=ops: [f(e) for f in ops])


class Ctr:
    def __init__(self):
        self.v = 0

    def inc(self, n=1):
        self.v += n
        return self.v


def build(upto="b4", dumps=()):
    nc = bass.Bass("TRN2", target_bir_lowering=False, debug=False)

    xT = nc.dram_tensor("xT", [H, TOK], F32, kind="ExternalInput").ap()
    wimg = {}
    for nm, nblk, wide in (("wq", QH, H), ("wk", KVH, H), ("wv", KVH, H),
                           ("wo", CH, H), ("wg", IT, H), ("wu", IT, H),
                           ("wd", CH, IT * 128)):
        wimg[nm] = nc.dram_tensor(nm, [nblk, 128, wide], F32, kind="ExternalInput").ap()
    bvecs = {}
    for nm, w in (("bq", QH), ("bk", KVH), ("bv", KVH), ("bo", CH),
                  ("ag", IT), ("au", IT), ("ad", CH)):
        bvecs[nm] = nc.dram_tensor(nm, [128, w], F32, kind="ExternalInput").ap()
    ident_in = nc.dram_tensor("ident", [128, 128], BF16, kind="ExternalInput").ap()
    outT = nc.dram_tensor("outT", [H, TOK], F32, kind="ExternalOutput").ap()
    x2d = nc.dram_tensor("x2d", [H, TOK], F32, kind="Internal").ap()
    cc_in = nc.dram_tensor("cc_in", [128, CCW], F32, kind="Internal").ap()
    cc_out = nc.dram_tensor("cc_out", [128, CCW], F32, kind="Internal").ap()

    sem_ms = nc.alloc_semaphore("s_ms")
    sem_vs = nc.alloc_semaphore("s_vs")
    sem_ss = nc.alloc_semaphore("s_ss")
    sem_cs = nc.alloc_semaphore("s_cs")
    MS, VS, SS = Ctr(), Ctr(), Ctr()
    semG = [nc.alloc_semaphore(f"s_g{i}") for i in range(4)]
    ctrG = [Ctr() for _ in range(4)]
    semX = [nc.alloc_semaphore(f"s_x{i}") for i in range(4)]
    ctrX = [Ctr() for _ in range(4)]
    semO = [nc.alloc_semaphore(f"s_o{i}") for i in range(2)]
    ctrO = [Ctr() for _ in range(2)]

    ident_sb = nc.alloc_sbuf_tensor("ident_sb", [128, 128], BF16)
    ones_col = nc.alloc_sbuf_tensor("ones_col", [128, 16], BF16)
    ones_row = nc.alloc_sbuf_tensor("ones_row", [1, 128], BF16)
    b_sb = {nm: nc.alloc_sbuf_tensor(nm + "_sb", [128, w], F32)
            for nm, w in (("bq", QH), ("bk", KVH), ("bv", KVH), ("bo", CH),
                          ("ag", IT), ("au", IT), ("ad", CH))}
    ln2a = [nc.alloc_sbuf_tensor(f"ln2a{t}", [1, TT], BF16) for t in range(NTB)]
    ln2b = [nc.alloc_sbuf_tensor(f"ln2b{t}", [1, TT], BF16) for t in range(NTB)]
    sc_t = nc.alloc_sbuf_tensor("sc_t", [128, 2 * TT], F32)
    sc_e = nc.alloc_sbuf_tensor("sc_e", [128, 2 * TT], F32)
    sc_r = nc.alloc_sbuf_tensor("sc_r", [128, 2 * TT], F32)
    sc_o = nc.alloc_sbuf_tensor("sc_o", [128, 2 * TT], F32)
    xre = nc.alloc_sbuf_tensor("xre", [128, 4 * TT], F32)
    wstream = nc.alloc_sbuf_tensor("wstream", [128, IT * 128], BF16)
    kvt_bf = nc.alloc_sbuf_tensor("kvt_bf", [128, KVH * 128], BF16)
    rn_sb = nc.alloc_sbuf_tensor("rn_sb", [128, KVH], F32)
    rn_tmp = nc.alloc_sbuf_tensor("rn_tmp", [128, KVH], F32)

    TOP = 229344
    MB_BYTES = 88064
    MA_BYTES = 65536
    MB0 = (TOP - MB_BYTES) // 32 * 32
    MA0 = (MB0 - MA_BYTES) // 32 * 32
    A = lambda name, shape, dt, off: nc.alloc_sbuf_tensor_at(name, shape, dt, offset=off)
    xln1 = A("xln1", [128, CH * TOK], BF16, MA0)
    xln2 = A("xln2", [128, CH * TT], BF16, MA0)
    gu = A("gu", [128, 4 * H], BF16, MA0 + 32768)
    xbf_r = A("xbf_r", [128, 4 * TT], BF16, MB0)
    xsq_r = A("xsq_r", [128, 4 * TT], BF16, MB0 + 4096)
    kphi = A("kphi", [128, KVH * TOK], BF16, MB0)
    vbf = A("vbf", [128, KVH * TOK], BF16, MB0 + 16384)
    kt_r = A("kt_r", [128, 4 * 128], BF16, MB0 + 32768)
    vt_r = A("vt_r", [128, 4 * 128], BF16, MB0 + 33792)
    qphi = A("qphi", [128, QH * TOK], BF16, MB0)
    h_sb = A("h_sb", [128, IT * TT], BF16, MB0)
    row_mean = A("row_mean", [1, TT], F32, MB0 + 65536)
    row_var = A("row_var", [1, TT], F32, MB0 + 67584)
    row_r = A("row_r", [1, TT], F32, MB0 + 69632)
    row_a_bf = A("row_a_bf", [1, TT], BF16, MB0 + 71680)
    row_b_bf = A("row_b_bf", [1, TT], BF16, MB0 + 72704)
    ccin_sb = A("ccin_sb", [128, CCW], F32, MB0 + 73728)
    ccout_sb = A("ccout_sb", [128, CCW], F32, MB0 + 77888)
    x2bf = A("x2bf", [128, 2 * TT], BF16, MB0 + 82048)
    x2sq = A("x2sq", [128, TT], BF16, MB0 + 84096)
    row_b = A("row_b", [1, TT], F32, MB0 + 85120)

    def sl(t, i, w):
        return t[:, i * w:(i + 1) * w]

    free = {}
    ready = {}
    xre_ctr = Ctr()

    def dma(P, eng, sems, ctrs, idx, dst, src):
        P.add(eng, lambda e, d=dst, s=src, sm=sems[idx]: e.dma_start(d, s).then_inc(sm, 16))
        return (sems[idx], ctrs[idx].inc(16))

    P = Prog()
    _b0_target = ctrX[0].v + 8 * 16
    def b0_sync(sy, v_target=_b0_target):
        sy.dma_start(ident_sb[:], ident_in[:]).then_inc(semX[0], 16)
        for nm in ("bq", "bk", "bv", "bo", "ag", "au", "ad"):
            sy.dma_start(b_sb[nm][:], bvecs[nm][:]).then_inc(semX[0], 16)
        sy.wait_ge(semX[0], v_target)
    P.add("sync", b0_sync)
    ctrX[0].inc(8 * 16)
    P.add("vector", lambda v: v.memset(ones_col[:], 1.0))
    P.add("vector", lambda v: v.memset(ones_row[:], 1.0))
    with nc.Block() as b0:
        P.emit(b0)

    def ln_finalize(P, s1_ps, s2_ps, a_dst, b_dst, stats_done_ms):
        inv = 1.0 / H
        P.wait("vector", sem_ms, stats_done_ms)
        P.add("vector", lambda v: v.tensor_scalar(row_mean[:], s1_ps[:], inv, None, ALU.mult))
        P.add("vector", lambda v: v.tensor_scalar(row_var[:], s2_ps[:], inv, None, ALU.mult)
              .then_inc(sem_vs, 1))
        v1 = VS.inc()
        P.wait("vector", sem_vs, v1)
        P.add("vector", lambda v: v.tensor_tensor(row_r[:], row_mean[:], row_mean[:], ALU.mult)
              .then_inc(sem_vs, 1))
        v2 = VS.inc()
        P.wait("vector", sem_vs, v2)
        P.add("vector", lambda v: v.tensor_tensor(row_var[:], row_var[:], row_r[:], ALU.subtract)
              .then_inc(sem_vs, 1))
        v_var = VS.inc()
        P.wait("scalar", sem_vs, v_var)
        P.add("scalar", lambda s: s.activation(row_b[:], row_var[:], AF.Ln,
                                               bias=np.float32(EPS_LN), scale=1.0)
              .then_inc(sem_ss, 1))
        s_ln = SS.inc()
        P.wait("scalar", sem_ss, s_ln)
        P.add("scalar", lambda s: s.activation(row_r[:], row_b[:], AF.Exp, bias=0.0, scale=-0.5)
              .then_inc(sem_ss, 1))
        s_r = SS.inc()
        P.wait("vector", sem_ss, s_r)
        P.add("vector", lambda v: v.scalar_tensor_tensor(row_b[:], row_mean[:], -1.0, row_r[:],
                                                         ALU.mult, ALU.mult).then_inc(sem_vs, 1))
        v4 = VS.inc()
        P.add("vector", lambda v, a=a_dst: v.tensor_copy(a[:], row_r[:]))
        P.wait("vector", sem_vs, v4)
        P.add("vector", lambda v, b=b_dst: v.tensor_copy(b[:], row_b[:]).then_inc(sem_vs, 1))
        return VS.inc()

    P = Prog()
    with (
        nc.psum_tensor("ps_s1", [1, TT], F32) as ps_s1,
        nc.psum_tensor("ps_s2", [1, TT], F32) as ps_s2,
        nc.psum_tensor("ps_abc", [128, TT], F32) as ps_abc,
        nc.psum_tensor("ps_bbc", [128, TT], F32) as ps_bbc,
    ):
        for tb in range(NTB):
            for c in range(CH):
                n = tb * CH + c
                gsl = n % 4
                if ("xbf", gsl) in free:
                    P.wait("gpsimd", *free[("xbf", gsl)])
                w_xbf = dma(P, "gpsimd", semG, ctrG, gsl, sl(xbf_r, gsl, TT),
                            xT[c * 128:(c + 1) * 128, tb * TT:(tb + 1) * TT])
                P.wait("scalar", *w_xbf)
                if ("xsq", gsl) in free:
                    P.wait("scalar", *free[("xsq", gsl)])
                P.add("scalar", lambda s, i=gsl:
                      s.activation(sl(xsq_r, i, TT), sl(xbf_r, i, TT), AF.Square)
                      .then_inc(sem_ss, 1))
                s_sq = SS.inc()
                P.wait("tensor", *w_xbf)
                P.wait("tensor", sem_ss, s_sq)
                if tb == 1 and c == 0:
                    P.wait("tensor", sem_vs, ready[("lnfin1", 0)])
                P.add("tensor", lambda t, i=gsl, st=(c == 0), sp=(c == CH - 1):
                      t.matmul(ps_s1[:], ones_col[:, 0:1], sl(xbf_r, i, TT), start=st, stop=sp))
                P.add("tensor", lambda t, i=gsl, st=(c == 0), sp=(c == CH - 1):
                      t.matmul(ps_s2[:], ones_col[:, 0:1], sl(xsq_r, i, TT), start=st, stop=sp)
                      .then_inc(sem_ms, 1))
                m_chunk = MS.inc()
                free[("xbf", gsl)] = (sem_ms, m_chunk)
                free[("xsq", gsl)] = (sem_ms, m_chunk)
            ready[("lnfin1", tb)] = VS.v + 1
            rows_ready = ln_finalize(P, ps_s1, ps_s2, row_a_bf, row_b_bf, m_chunk)
            P.wait("tensor", sem_vs, rows_ready)
            if tb == 1:
                P.wait("tensor", sem_vs, ready[("applydone", 0)])
            P.add("tensor", lambda t: t.matmul(ps_abc[:], ones_row[:], row_a_bf[:],
                                               start=True, stop=True))
            P.add("tensor", lambda t: t.matmul(ps_bbc[:], ones_row[:], row_b_bf[:],
                                               start=True, stop=True).then_inc(sem_ms, 1))
            m_bc = MS.inc()
            P.wait("vector", sem_ms, m_bc)
            for c in range(CH):
                xsl = xre_ctr.v % 4
                xre_ctr.inc()
                if ("xre", xsl) in free:
                    P.wait("sync", *free[("xre", xsl)])
                x_ld = dma(P, "sync", semX, ctrX, xsl, sl(xre, xsl, TT),
                           xT[c * 128:(c + 1) * 128, tb * TT:(tb + 1) * TT])
                P.wait("vector", *x_ld)
                P.add("vector", lambda v, s=xsl, c=c:
                      v.tensor_tensor(sl(sc_t, c % 2, TT), sl(xre, s, TT), ps_abc[:], ALU.mult)
                      .then_inc(sem_vs, 1))
                v_t = VS.inc()
                P.wait("vector", sem_vs, v_t)
                P.add("vector", lambda v, c=c, tb=tb:
                      v.tensor_tensor(xln1[:, c * TOK + tb * TT: c * TOK + (tb + 1) * TT],
                                      sl(sc_t, c % 2, TT), ps_bbc[:], ALU.add)
                      .then_inc(sem_vs, 1))
                v_ap = VS.inc()
                free[("xre", xsl)] = (sem_vs, v_ap)
            ready[("applydone", tb)] = v_ap
        with nc.Block() as b1:
            P.emit(b1)

    if upto == "b1":
        return nc, _emit_dumps(nc, dumps, locals())

    tile = Ctr()
    widx = Ctr()

    def w_block_dma(P, img, j, width=H):
        s = widx.v % 2
        if ("wslot", s) in free:
            P.wait("gpsimd", *free[("wslot", s)])
        pair = dma(P, "gpsimd", semG, ctrG, s,
                   wstream[:, s * H:s * H + width], img[j])
        return pair, s

    def proj_block(P, ps_ring, img, j, bias_col, kind, out_sb, out_off):
        wv, ws_slot = w_block_dma(P, img, j)
        widx.inc()
        for tb in range(NTB):
            t = tile.v
            tile.inc()
            slot = t % 4
            P.wait("tensor", *wv)
            if ("ps", slot) in free:
                P.wait("tensor", *free[("ps", slot)])
            for c in range(CH):
                st, sp = (c == 0), (c == CH - 1)
                if sp:
                    P.add("tensor", lambda te, c=c, s=slot, j2=ws_slot, tb=tb:
                          te.matmul(sl(ps_ring, s, TT),
                                    wstream[:, j2 * H + c * 128: j2 * H + (c + 1) * 128],
                                    xln1[:, c * TOK + tb * TT: c * TOK + (tb + 1) * TT],
                                    start=False, stop=True).then_inc(sem_ms, 1))
                else:
                    P.add("tensor", lambda te, c=c, s=slot, j2=ws_slot, tb=tb, st=st:
                          te.matmul(sl(ps_ring, s, TT),
                                    wstream[:, j2 * H + c * 128: j2 * H + (c + 1) * 128],
                                    xln1[:, c * TOK + tb * TT: c * TOK + (tb + 1) * TT],
                                    start=st, stop=False))
            m_grp = MS.inc()
            dst = out_sb[:, out_off + tb * TT: out_off + (tb + 1) * TT]
            if kind == "phi":
                P.wait("vector", sem_ms, m_grp)
                if ("sct", t % 2) in free:
                    P.wait("vector", *free[("sct", t % 2)])
                P.add("vector", lambda v, s=slot, i=t % 2, b=bias_col:
                      v.tensor_scalar(sl(sc_t, i, TT), sl(ps_ring, s, TT), b, 0.0,
                                      ALU.add, ALU.min))
                P.add("vector", lambda v, s=slot, i=t % 2, b=bias_col:
                      v.tensor_scalar(sl(sc_r, i, TT), sl(ps_ring, s, TT), b, 0.0,
                                      ALU.add, ALU.max).then_inc(sem_vs, 1))
                v_mm = VS.inc()
                P.wait("scalar", sem_vs, v_mm)
                if ("sce", t % 2) in free:
                    P.wait("scalar", *free[("sce", t % 2)])
                P.add("scalar", lambda s, i=t % 2:
                      s.activation(sl(sc_e, i, TT), sl(sc_t, i, TT), AF.Exp).then_inc(sem_ss, 1))
                s_exp = SS.inc()
                free[("sct", t % 2)] = (sem_ss, s_exp)
                P.wait("vector", sem_ss, s_exp)
                P.wait("vector", sem_vs, v_mm)
                P.add("vector", lambda v, i=t % 2, d=dst:
                      v.tensor_tensor(d, sl(sc_e, i, TT), sl(sc_r, i, TT), ALU.add)
                      .then_inc(sem_vs, 1))
                v_done = VS.inc()
                free[("sce", t % 2)] = (sem_vs, v_done)
                free[("ps", slot)] = (sem_vs, v_done)
                ready[("tile", t)] = (sem_vs, v_done)
            else:
                P.wait("scalar", sem_ms, m_grp)
                P.add("scalar", lambda s, sp2=slot, b=bias_col, d=dst:
                      s.activation(d, sl(ps_ring, sp2, TT), AF.Identity, bias=b, scale=1.0)
                      .then_inc(sem_ss, 1))
                s_done = SS.inc()
                free[("ps", slot)] = (sem_ss, s_done)
                ready[("tile", t)] = (sem_ss, s_done)
        free[("wslot", ws_slot)] = (sem_ms, m_grp)

    P = Prog()
    with (
        nc.psum_tensor("ps_ringA", [128, 4 * TT], F32) as ps_ringA,
        nc.psum_tensor("ps_tpA", [128, 128], BF16) as ps_tpA,
        nc.psum_tensor("ps_tpB", [128, 128], BF16) as ps_tpB,
        nc.psum_tensor("ps_kvA", [128, 128], F32) as ps_kvA,
        nc.psum_tensor("ps_kvB", [128, 128], F32) as ps_kvB,
    ):
        k_tiles = []
        for j in range(KVH):
            proj_block(P, ps_ringA, wimg["wk"], j, b_sb["bk"][:, j:j + 1], "phi", kphi, j * TOK)
            k_tiles += [tile.v - 2, tile.v - 1]
        v_tiles = []
        for j in range(KVH):
            proj_block(P, ps_ringA, wimg["wv"], j, b_sb["bv"][:, j:j + 1], "id", vbf, j * TOK)
            v_tiles += [tile.v - 2, tile.v - 1]
        P.wait("tensor", *ready[("tile", k_tiles[-1])])
        P.wait("tensor", *ready[("tile", v_tiles[-1])])
        kv_ps = [ps_kvA, ps_kvB]
        mkv = {}
        for hh in range(KVH):
            for c in range(8):
                i = hh * 8 + c
                if ("tp",) in free:
                    P.wait("tensor", *free[("tp",)])
                P.add("tensor", lambda te, hh=hh, c=c:
                      te.transpose(ps_tpA[:],
                                   kphi[:, hh * TOK + c * 128: hh * TOK + (c + 1) * 128],
                                   ident_sb[:]).then_inc(sem_ms, 1))
                m_tk = MS.inc()
                P.add("tensor", lambda te, hh=hh, c=c:
                      te.transpose(ps_tpB[:],
                                   vbf[:, hh * TOK + c * 128: hh * TOK + (c + 1) * 128],
                                   ident_sb[:]).then_inc(sem_ms, 1))
                m_tv = MS.inc()
                P.wait("vector", sem_ms, m_tk)
                if i >= 4:
                    P.wait("vector", sem_ms, mkv[i - 4])
                P.add("vector", lambda v, i=i: v.tensor_copy(sl(kt_r, i % 4, 128), ps_tpA[:]))
                P.wait("vector", sem_ms, m_tv)
                P.add("vector", lambda v, i=i: v.tensor_copy(sl(vt_r, i % 4, 128), ps_tpB[:])
                      .then_inc(sem_vs, 1))
                v_tp = VS.inc()
                free[("tp",)] = (sem_vs, v_tp)
                P.wait("tensor", sem_vs, v_tp)
                if c == 0 and ("kvps", hh % 2) in free:
                    P.wait("tensor", *free[("kvps", hh % 2)])
                st, sp = (c == 0), (c == 7)
                P.add("tensor", lambda te, hh=hh, i=i, st=st, sp=sp:
                      te.matmul(kv_ps[hh % 2][:], sl(kt_r, i % 4, 128), sl(vt_r, i % 4, 128),
                                start=st, stop=sp).then_inc(sem_ms, 1))
                mkv[i] = MS.inc()
                if sp:
                    m_kv = mkv[i]
            P.wait("vector", sem_ms, m_kv)
            P.add("vector", lambda v, hh=hh:
                  v.tensor_copy(ccin_sb[:, hh * 128:(hh + 1) * 128], kv_ps[hh % 2][:])
                  .then_inc(sem_vs, 1))
            free[("kvps", hh % 2)] = (sem_vs, VS.inc())
        for hh in range(KVH):
            if hh == KVH - 1:
                P.add("vector", lambda v, hh=hh:
                      v.reduce_sum(ccin_sb[:, KVH * 128 + hh: KVH * 128 + hh + 1],
                                   kphi[:, hh * TOK:(hh + 1) * TOK], axis=AX.X)
                      .then_inc(sem_vs, 1))
            else:
                P.add("vector", lambda v, hh=hh:
                      v.reduce_sum(ccin_sb[:, KVH * 128 + hh: KVH * 128 + hh + 1],
                                   kphi[:, hh * TOK:(hh + 1) * TOK], axis=AX.X))
        v_stage = VS.inc()
        P.wait("sync", sem_vs, v_stage)
        o_stage = dma(P, "sync", semO, ctrO, 0, cc_in[:], ccin_sb[:])
        P.wait("gpsimd", *o_stage)
        P.add("gpsimd", lambda g: g.collective_compute(
            "AllReduce", ALU.add, replica_groups=[[0, 1, 2, 3], [4, 5, 6, 7]],
            ins=[cc_in[:]], outs=[cc_out[:]]).then_inc(sem_cs, 1))
        with nc.Block() as b2a:
            P.emit(b2a)

    if upto == "b2a":
        return nc, _emit_dumps(nc, dumps, locals())

    P = Prog()
    with nc.psum_tensor("ps_ringB", [128, 4 * TT], F32) as ps_ringB:
        for k in range(4):
            free.pop(("ps", k), None)
        P.wait("sync", sem_cs, 1)
        x_cc = dma(P, "sync", semX, ctrX, 0, ccout_sb[:], cc_out[:])
        q_tiles = []
        post = [False]
        for j in range(QH):
            proj_block(P, ps_ringB, wimg["wq"], j, b_sb["bq"][:, j:j + 1], "phi", qphi, j * TOK)
            q_tiles += [tile.v - 2, tile.v - 1]
            if j == 3 and not post[0]:
                P.wait("vector", *x_cc)
                P.add("vector", lambda v: v.tensor_copy(kvt_bf[:], ccout_sb[:, :KVH * 128]))
                P.add("vector", lambda v: v.tensor_scalar(rn_tmp[:], ccout_sb[:, KVH * 128:],
                                                          EPS_ATT, None, ALU.add)
                      .then_inc(sem_vs, 1))
                v_rt = VS.inc()
                P.wait("vector", sem_vs, v_rt)
                P.add("vector", lambda v: v.reciprocal(rn_sb[:], rn_tmp[:]).then_inc(sem_vs, 1))
                ready[("kvrdy",)] = (sem_vs, VS.inc())
                post[0] = True
        P.wait("tensor", *ready[("kvrdy",)])
        for qh in range(QH):
            kvh = qh // 4
            for tb in range(NTB):
                t = tile.v
                tile.inc()
                slot = t % 4
                P.wait("tensor", *ready[("tile", q_tiles[qh * 2 + tb])])
                if ("ps", slot) in free:
                    P.wait("tensor", *free[("ps", slot)])
                P.add("tensor", lambda te, qh=qh, kvh=kvh, tb=tb, s=slot:
                      te.matmul(sl(ps_ringB, s, TT), kvt_bf[:, kvh * 128:(kvh + 1) * 128],
                                qphi[:, qh * TOK + tb * TT: qh * TOK + (tb + 1) * TT],
                                start=True, stop=True).then_inc(sem_ms, 1))
                m_at = MS.inc()
                P.wait("scalar", *ready[("kvrdy",)])
                P.wait("scalar", sem_ms, m_at)
                P.add("scalar", lambda s, qh=qh, kvh=kvh, tb=tb, sp=slot:
                      s.activation(qphi[:, qh * TOK + tb * TT: qh * TOK + (tb + 1) * TT],
                                   sl(ps_ringB, sp, TT), AF.Identity,
                                   bias=0.0, scale=rn_sb[:, kvh:kvh + 1]).then_inc(sem_ss, 1))
                s_at = SS.inc()
                free[("ps", slot)] = (sem_ss, s_at)
                ready[("attn", qh * 2 + tb)] = (sem_ss, s_at)
        with nc.Block() as b2b:
            P.emit(b2b)

    if upto == "b2b":
        return nc, _emit_dumps(nc, dumps, locals())

    P = Prog()
    with (
        nc.psum_tensor("ps_ringC", [128, 4 * TT], F32) as ps_ringC,
        nc.psum_tensor("ps2_s1a", [1, TT], F32) as ps2_s1a,
        nc.psum_tensor("ps2_s2a", [1, TT], F32) as ps2_s2a,
        nc.psum_tensor("ps2_s1b", [1, TT], F32) as ps2_s1b,
        nc.psum_tensor("ps2_s2b", [1, TT], F32) as ps2_s2b,
    ):
        for k in range(4):
            free.pop(("ps", k), None)
        P.wait("tensor", *ready[("attn", 63)])
        st_ps = [(ps2_s1a, ps2_s2a), (ps2_s1b, ps2_s2b)]
        x2bf_rdy, x2sq_rdy = {}, {}
        for s in range(CH * NTB):
            j, tb = s // 2, s % 2
            wv, ws_slot = w_block_dma(P, wimg["wo"], j)
            widx.inc()
            t = tile.v
            tile.inc()
            slot = t % 4
            if s >= 1:
                sp_, jp, tbp = s - 1, (s - 1) // 2, (s - 1) % 2
                P.wait("tensor", sem_vs, x2bf_rdy[sp_])
                P.wait("tensor", sem_ss, x2sq_rdy[sp_])
                P.add("tensor", lambda te, tbp=tbp, i=sp_ % 2, st=(jp == 0), sp2=(jp == CH - 1):
                      te.matmul(st_ps[tbp][0][:], ones_col[:, 0:1], sl(x2bf, i, TT),
                                start=st, stop=sp2))
                P.add("tensor", lambda te, tbp=tbp, st=(jp == 0), sp2=(jp == CH - 1):
                      te.matmul(st_ps[tbp][1][:], ones_col[:, 0:1], x2sq[:],
                                start=st, stop=sp2).then_inc(sem_ms, 1))
                ready[("x2stat", sp_)] = MS.inc()
            P.wait("tensor", *wv)
            if ("ps", slot) in free:
                P.wait("tensor", *free[("ps", slot)])
            for c in range(CH):
                st, sp2 = (c == 0), (c == CH - 1)
                if sp2:
                    P.add("tensor", lambda te, c=c, s2=slot, j2=ws_slot, tb=tb:
                          te.matmul(sl(ps_ringC, s2, TT),
                                    wstream[:, j2 * H + c * 128: j2 * H + (c + 1) * 128],
                                    qphi[:, c * TOK + tb * TT: c * TOK + (tb + 1) * TT],
                                    start=False, stop=True).then_inc(sem_ms, 1))
                else:
                    P.add("tensor", lambda te, c=c, s2=slot, j2=ws_slot, tb=tb, st=st:
                          te.matmul(sl(ps_ringC, s2, TT),
                                    wstream[:, j2 * H + c * 128: j2 * H + (c + 1) * 128],
                                    qphi[:, c * TOK + tb * TT: c * TOK + (tb + 1) * TT],
                                    start=st, stop=False))
            m_grp = MS.inc()
            free[("wslot", ws_slot)] = (sem_ms, m_grp)
            xsl = xre_ctr.v % 4
            xre_ctr.inc()
            if ("xre", xsl) in free:
                P.wait("sync", *free[("xre", xsl)])
            x_ld = dma(P, "sync", semX, ctrX, xsl, sl(xre, xsl, TT),
                       xT[j * 128:(j + 1) * 128, tb * TT:(tb + 1) * TT])
            P.wait("scalar", sem_ms, m_grp)
            if ("sct", s % 2) in free:
                P.wait("scalar", *free[("sct", s % 2)])
            P.add("scalar", lambda sc, s2=slot, j=j, i=s % 2:
                  sc.activation(sl(sc_t, i, TT), sl(ps_ringC, s2, TT), AF.Identity,
                                bias=b_sb["bo"][:, j:j + 1], scale=1.0).then_inc(sem_ss, 1))
            s_id = SS.inc()
            free[("ps", slot)] = (sem_ss, s_id)
            P.wait("vector", sem_ss, s_id)
            P.wait("vector", *x_ld)
            if ("sco", s % 2) in free:
                P.wait("vector", *free[("sco", s % 2)])
            P.add("vector", lambda v, i=s % 2, xs=xsl:
                  v.tensor_tensor(sl(sc_o, i, TT), sl(sc_t, i, TT), sl(xre, xs, TT), ALU.add)
                  .then_inc(sem_vs, 1))
            v_x2 = VS.inc()
            free[("sct", s % 2)] = (sem_vs, v_x2)
            free[("xre", xsl)] = (sem_vs, v_x2)
            P.wait("scalar", sem_vs, v_x2)
            if s >= 1:
                P.wait("scalar", sem_ms, ready[("x2stat", s - 1)])
            P.add("scalar", lambda sc, i=s % 2:
                  sc.activation(x2sq[:], sl(sc_o, i, TT), AF.Square).then_inc(sem_ss, 1))
            x2sq_rdy[s] = SS.inc()
            if s >= 2:
                P.wait("vector", sem_ms, ready[("x2stat", s - 2)])
            P.wait("vector", sem_vs, v_x2)
            P.add("vector", lambda v, i=s % 2:
                  v.tensor_copy(sl(x2bf, i, TT), sl(sc_o, i, TT)).then_inc(sem_vs, 1))
            x2bf_rdy[s] = VS.inc()
            P.wait("sync", sem_vs, v_x2)
            o_x2 = dma(P, "sync", semO, ctrO, s % 2,
                       x2d[j * 128:(j + 1) * 128, tb * TT:(tb + 1) * TT], sl(sc_o, i := s % 2, TT))
            free[("sco", s % 2)] = o_x2
        sp_ = CH * NTB - 1
        tbp = sp_ % 2
        P.wait("tensor", sem_vs, x2bf_rdy[sp_])
        P.wait("tensor", sem_ss, x2sq_rdy[sp_])
        P.add("tensor", lambda te, tbp=tbp, i=sp_ % 2:
              te.matmul(st_ps[tbp][0][:], ones_col[:, 0:1], sl(x2bf, i, TT),
                        start=False, stop=True))
        P.add("tensor", lambda te, tbp=tbp:
              te.matmul(st_ps[tbp][1][:], ones_col[:, 0:1], x2sq[:],
                        start=False, stop=True).then_inc(sem_ms, 1))
        m_last = MS.inc()

        def ln2_fin(P, tb, stats_ms):
            inv = 1.0 / H
            P.wait("vector", sem_ms, stats_ms)
            P.add("vector", lambda v, tb=tb:
                  v.tensor_scalar(row_mean[:], st_ps[tb][0][:], inv, None, ALU.mult))
            P.add("vector", lambda v, tb=tb:
                  v.tensor_scalar(row_var[:], st_ps[tb][1][:], inv, None, ALU.mult)
                  .then_inc(sem_vs, 1))
            v1 = VS.inc()
            P.wait("vector", sem_vs, v1)
            P.add("vector", lambda v: v.tensor_tensor(row_r[:], row_mean[:], row_mean[:], ALU.mult)
                  .then_inc(sem_vs, 1))
            v2 = VS.inc()
            P.wait("vector", sem_vs, v2)
            P.add("vector", lambda v: v.tensor_tensor(row_var[:], row_var[:], row_r[:], ALU.subtract)
                  .then_inc(sem_vs, 1))
            v_var = VS.inc()
            P.wait("scalar", sem_vs, v_var)
            P.add("scalar", lambda sc: sc.activation(row_b[:], row_var[:], AF.Ln,
                                                     bias=np.float32(EPS_LN), scale=1.0)
                  .then_inc(sem_ss, 1))
            s_ln = SS.inc()
            P.wait("scalar", sem_ss, s_ln)
            P.add("scalar", lambda sc: sc.activation(row_r[:], row_b[:], AF.Exp, bias=0.0, scale=-0.5)
                  .then_inc(sem_ss, 1))
            s_r = SS.inc()
            P.wait("vector", sem_ss, s_r)
            P.add("vector", lambda v: v.scalar_tensor_tensor(row_b[:], row_mean[:], -1.0, row_r[:],
                                                             ALU.mult, ALU.mult).then_inc(sem_vs, 1))
            v4 = VS.inc()
            P.add("vector", lambda v, tb=tb: v.tensor_copy(ln2a[tb][:], row_r[:]))
            P.wait("vector", sem_vs, v4)
            P.add("vector", lambda v, tb=tb: v.tensor_copy(ln2b[tb][:], row_b[:]).then_inc(sem_vs, 1))
            return VS.inc()

        ln2_fin(P, 0, ready[("x2stat", CH * NTB - 2)])
        ln2_fin(P, 1, m_last)
        P.add("sync", lambda sy, v0=ctrO[0].v, v1=ctrO[1].v:
              (sy.wait_ge(semO[0], v0), sy.wait_ge(semO[1], v1)))
        with nc.Block() as b3:
            P.emit(b3)

    if upto == "b3":
        return nc, _emit_dumps(nc, dumps, locals())

    P = Prog()
    with (
        nc.psum_tensor("ps_a2", [128, TT], F32) as ps_a2,
        nc.psum_tensor("ps_b2", [128, TT], F32) as ps_b2,
        nc.psum_tensor("ps_ringD", [128, 4 * TT], F32) as ps_ringD,
    ):
        for k in range(4):
            free.pop(("ps", k), None)
        free.pop(("wslot", 0), None)
        free.pop(("wslot", 1), None)
        gtile = Ctr()
        last_up, last_dn = {}, {}
        for tb in range(NTB):
            if tb == 1:
                P.wait("tensor", sem_vs, ready[("ffnapply", 0)])
            P.add("tensor", lambda te, tb=tb: te.matmul(ps_a2[:], ones_row[:], ln2a[tb][:],
                                                        start=True, stop=True))
            P.add("tensor", lambda te, tb=tb: te.matmul(ps_b2[:], ones_row[:], ln2b[tb][:],
                                                        start=True, stop=True).then_inc(sem_ms, 1))
            m_bc = MS.inc()
            P.wait("vector", sem_ms, m_bc)
            if tb == 1:
                P.wait("vector", sem_ms, last_up[0])
            apply_v = {}
            for c in range(CH):
                xsl = xre_ctr.v % 4
                xre_ctr.inc()
                if ("xre", xsl) in free:
                    P.wait("sync", *free[("xre", xsl)])
                x_ld = dma(P, "sync", semX, ctrX, xsl, sl(xre, xsl, TT),
                           x2d[c * 128:(c + 1) * 128, tb * TT:(tb + 1) * TT])
                P.wait("vector", *x_ld)
                if ("sct", c % 2) in free:
                    P.wait("vector", *free[("sct", c % 2)])
                P.add("vector", lambda v, i=xsl, c=c:
                      v.tensor_tensor(sl(sc_t, c % 2, TT), sl(xre, i, TT), ps_a2[:], ALU.mult)
                      .then_inc(sem_vs, 1))
                v_t = VS.inc()
                P.wait("vector", sem_vs, v_t)
                P.add("vector", lambda v, c=c:
                      v.tensor_tensor(xln2[:, c * TT:(c + 1) * TT], sl(sc_t, c % 2, TT),
                                      ps_b2[:], ALU.add).then_inc(sem_vs, 1))
                v_ap = VS.inc()
                apply_v[c] = v_ap
                free[("xre", xsl)] = (sem_vs, v_ap)
                free[("sct", c % 2)] = (sem_vs, v_ap)
            ready[("ffnapply", tb)] = apply_v[CH - 1]
            h_rdy = {}
            for jt in range(IT):
                guw = {}
                for st_i, img in ((0, wimg["wg"]), (1, wimg["wu"])):
                    gslot = st_i * 2 + (jt % 2)
                    if ("gu", gslot) in free:
                        P.wait("gpsimd", *free[("gu", gslot)])
                    if ("wd", gslot) in free:
                        P.wait("gpsimd", *free[("wd", gslot)])
                    guw[st_i] = dma(P, "gpsimd", semG, ctrG, gslot, sl(gu, gslot, H), img[jt])
                t = gtile.v; gtile.inc()
                slot = t % 4
                P.wait("tensor", *guw[0])
                if ("psD", slot) in free:
                    P.wait("tensor", *free[("psD", slot)])
                for c in range(CH):
                    if jt == 0:
                        P.wait("tensor", sem_vs, apply_v[c])
                    st, sp = (c == 0), (c == CH - 1)
                    gs = 0 * 2 + jt % 2
                    if sp:
                        P.add("tensor", lambda te, c=c, s=slot, gs=gs:
                              te.matmul(sl(ps_ringD, s, TT),
                                        gu[:, gs * H + c * 128: gs * H + (c + 1) * 128],
                                        xln2[:, c * TT:(c + 1) * TT],
                                        start=False, stop=True).then_inc(sem_ms, 1))
                    else:
                        P.add("tensor", lambda te, c=c, s=slot, gs=gs, st=st:
                              te.matmul(sl(ps_ringD, s, TT),
                                        gu[:, gs * H + c * 128: gs * H + (c + 1) * 128],
                                        xln2[:, c * TT:(c + 1) * TT],
                                        start=st, stop=False))
                m_gate = MS.inc()
                free[("gu", 0 * 2 + jt % 2)] = (sem_ms, m_gate)
                t2 = gtile.v; gtile.inc()
                slot2 = t2 % 4
                P.wait("tensor", *guw[1])
                if ("psD", slot2) in free:
                    P.wait("tensor", *free[("psD", slot2)])
                for c in range(CH):
                    st, sp = (c == 0), (c == CH - 1)
                    gs = 1 * 2 + jt % 2
                    if sp:
                        P.add("tensor", lambda te, c=c, s=slot2, gs=gs:
                              te.matmul(sl(ps_ringD, s, TT),
                                        gu[:, gs * H + c * 128: gs * H + (c + 1) * 128],
                                        xln2[:, c * TT:(c + 1) * TT],
                                        start=False, stop=True).then_inc(sem_ms, 1))
                    else:
                        P.add("tensor", lambda te, c=c, s=slot2, gs=gs, st=st:
                              te.matmul(sl(ps_ringD, s, TT),
                                        gu[:, gs * H + c * 128: gs * H + (c + 1) * 128],
                                        xln2[:, c * TT:(c + 1) * TT],
                                        start=st, stop=False))
                m_up = MS.inc()
                last_up[tb] = m_up
                free[("gu", 1 * 2 + jt % 2)] = (sem_ms, m_up)
                P.wait("scalar", sem_ms, m_gate)
                if ("scr", jt % 2) in free:
                    P.wait("scalar", *free[("scr", jt % 2)])
                P.add("scalar", lambda s, sp=slot, jt=jt, i=jt % 2:
                      s.activation(sl(sc_r, i, TT), sl(ps_ringD, sp, TT), AF.Silu,
                                   bias=0.0, scale=b_sb["ag"][:, jt:jt + 1]).then_inc(sem_ss, 1))
                s_silu = SS.inc()
                free[("psD", slot)] = (sem_ss, s_silu)
                P.wait("vector", sem_ms, m_up)
                P.add("vector", lambda v, sp=slot2, jt=jt, i=jt % 2:
                      v.tensor_scalar(sl(sc_e, i, TT), sl(ps_ringD, sp, TT),
                                      b_sb["au"][:, jt:jt + 1], None, ALU.mult)
                      .then_inc(sem_vs, 1))
                v_u = VS.inc()
                P.wait("vector", sem_ss, s_silu)
                P.wait("vector", sem_vs, v_u)
                if tb == 1 and jt == 0:
                    P.wait("vector", sem_ms, last_dn[0])
                P.add("vector", lambda v, jt=jt, i=jt % 2:
                      v.tensor_tensor(h_sb[:, jt * TT:(jt + 1) * TT], sl(sc_r, i, TT),
                                      sl(sc_e, i, TT), ALU.mult).then_inc(sem_vs, 1))
                v_h = VS.inc()
                h_rdy[jt] = v_h
                free[("psD", slot2)] = (sem_vs, v_h)
                free[("scr", jt % 2)] = (sem_vs, v_h)
            P.wait("tensor", sem_vs, h_rdy[IT - 1])
            for o in range(CH):
                wdv = []
                for half in range(2):
                    if ("wd", half) in free:
                        P.wait("gpsimd", *free[("wd", half)])
                    if ("gu", half) in free:
                        P.wait("gpsimd", *free[("gu", half)])
                    wdv.append(dma(P, "gpsimd", semG, ctrG, half,
                                   wstream[:, half * 5504:(half + 1) * 5504],
                                   wimg["wd"][o][:, half * 5504:(half + 1) * 5504]))
                t = gtile.v; gtile.inc()
                slot = t % 4
                if ("psD", slot) in free:
                    P.wait("tensor", *free[("psD", slot)])
                for c in range(IT):
                    half = 0 if c < 43 else 1
                    if c == 0:
                        P.wait("tensor", *wdv[0])
                    if c == 43:
                        P.wait("tensor", *wdv[1])
                    cc = c - 43 * half
                    st, sp = (c == 0), (c == IT - 1)
                    if sp or c == 42:
                        P.add("tensor", lambda te, c=c, cc=cc, half=half, s=slot, sp=sp, st=st:
                              te.matmul(sl(ps_ringD, s, TT),
                                        wstream[:, half * 5504 + cc * 128: half * 5504 + (cc + 1) * 128],
                                        h_sb[:, c * TT:(c + 1) * TT],
                                        start=st, stop=sp).then_inc(sem_ms, 1))
                    else:
                        P.add("tensor", lambda te, c=c, cc=cc, half=half, s=slot, st=st:
                              te.matmul(sl(ps_ringD, s, TT),
                                        wstream[:, half * 5504 + cc * 128: half * 5504 + (cc + 1) * 128],
                                        h_sb[:, c * TT:(c + 1) * TT],
                                        start=st, stop=False))
                    if c == 42:
                        free[("wd", 0)] = (sem_ms, MS.inc())
                m_dn = MS.inc()
                last_dn[tb] = m_dn
                free[("wd", 1)] = (sem_ms, m_dn)
                P.wait("scalar", sem_ms, m_dn)
                if ("sct", o % 2) in free:
                    P.wait("scalar", *free[("sct", o % 2)])
                P.add("scalar", lambda s, sp=slot, o=o, i=o % 2:
                      s.activation(sl(sc_t, i, TT), sl(ps_ringD, sp, TT), AF.Identity,
                                   bias=0.0, scale=b_sb["ad"][:, o:o + 1]).then_inc(sem_ss, 1))
                s_dn = SS.inc()
                free[("psD", slot)] = (sem_ss, s_dn)
                xsl = xre_ctr.v % 4
                xre_ctr.inc()
                if ("xre", xsl) in free:
                    P.wait("sync", *free[("xre", xsl)])
                x_ld = dma(P, "sync", semX, ctrX, xsl, sl(xre, xsl, TT),
                           x2d[o * 128:(o + 1) * 128, tb * TT:(tb + 1) * TT])
                P.wait("vector", sem_ss, s_dn)
                P.wait("vector", *x_ld)
                if ("sco", o % 2) in free:
                    P.wait("vector", *free[("sco", o % 2)])
                P.add("vector", lambda v, i=o % 2, xs=xsl:
                      v.tensor_tensor(sl(sc_o, i, TT), sl(sc_t, i, TT), sl(xre, xs, TT), ALU.add)
                      .then_inc(sem_vs, 1))
                v_out = VS.inc()
                free[("sct", o % 2)] = (sem_vs, v_out)
                free[("xre", xsl)] = (sem_vs, v_out)
                P.wait("sync", sem_vs, v_out)
                o_out = dma(P, "sync", semO, ctrO, o % 2,
                            outT[o * 128:(o + 1) * 128, tb * TT:(tb + 1) * TT], sl(sc_o, o % 2, TT))
                free[("sco", o % 2)] = o_out
        P.add("sync", lambda sy, v0=ctrO[0].v, v1=ctrO[1].v:
              (sy.wait_ge(semO[0], v0), sy.wait_ge(semO[1], v1)))
        with nc.Block() as b4:
            P.emit(b4)

    return nc, _emit_dumps(nc, dumps, locals())


def _emit_dumps(nc, dumps, env):
    if not dumps:
        return []
    names = []
    with nc.semaphore("dmp") as dsem, nc.Block() as bd:
        def _(sy):
            n = 0
            for nm in dumps:
                src = env[nm]
                ap = src if isinstance(src, bass.AP) else src.ap() if hasattr(src, "ap") else src[:]
                shape = list(ap.tensor.shape) if hasattr(ap, "tensor") else list(src.shape)
                dt = ap.dtype
                out = nc.dram_tensor("dump_" + nm, shape, dt, kind="ExternalOutput").ap()
                sy.dma_start(out[:], ap).then_inc(dsem, 16)
                n += 16
                names.append("dump_" + nm)
            sy.wait_ge(dsem, n)
        bd.sync(_)
    return names


_NC_CACHE = {}


def _prep_inputs(inputs):
    x = np.asarray(inputs["L0_attn_hidden_states"], dtype=np.float32)
    xf = np.ascontiguousarray(x.reshape(2 * 4096, H))

    def img(w, nblk, wide):
        w = np.asarray(w, dtype=np.float32)
        return np.ascontiguousarray(
            w.reshape(nblk, 128, wide // 128, 128).transpose(0, 3, 2, 1).reshape(nblk, 128, wide))

    def cols(v, n):
        return np.ascontiguousarray(np.asarray(v, dtype=np.float32).reshape(n, 128).T)

    common = {
        "wq": img(inputs["L0_attn_Wq"], QH, H),
        "wk": img(inputs["L0_attn_Wk"], KVH, H),
        "wv": img(inputs["L0_attn_Wv"], KVH, H),
        "wo": img(inputs["L0_attn_Wo"], CH, H),
        "wg": img(inputs["L0_ffn_Wg"], IT, H),
        "wu": img(inputs["L0_ffn_Wu"], IT, H),
        "wd": img(inputs["L0_ffn_Wd"], CH, IT * 128),
        "bq": cols(inputs["L0_attn_bq"], QH),
        "bk": cols(inputs["L0_attn_bk"], KVH),
        "bv": cols(inputs["L0_attn_bv"], KVH),
        "bo": cols(inputs["L0_attn_bo"], CH),
        "ag": cols(inputs["L0_ffn_ag"], IT),
        "au": cols(inputs["L0_ffn_au"], IT),
        "ad": cols(inputs["L0_ffn_ad"], CH),
        "ident": np.eye(128, dtype=ml_dtypes.bfloat16),
    }
    in_maps = []
    for c in range(8):
        m = dict(common)
        m["xT"] = np.ascontiguousarray(xf[c * TOK:(c + 1) * TOK].T)
        in_maps.append(m)
    return in_maps


def kernel(**inputs):
    in_maps = _prep_inputs(inputs)
    if "nc" not in _NC_CACHE:
        _NC_CACHE["nc"] = build()[0]
    nc = _NC_CACHE["nc"]
    res = run_bass_kernel_spmd(nc, in_maps, core_ids=list(range(8)))
    parts = [res.results[c]["outT"].T for c in range(8)]
    out = np.concatenate(parts, axis=0).reshape(2, 4096, H)
    return np.ascontiguousarray(out, dtype=np.float32)


# version 03
# speedup vs baseline: 1.0459x; 1.0459x over previous
import numpy as np
import ml_dtypes
import concourse.bass as bass
import concourse.mybir as mybir
from concourse.bass_utils import run_bass_kernel_spmd

F32 = mybir.dt.float32
BF16 = mybir.dt.bfloat16
AF = mybir.ActivationFunctionType
ALU = mybir.AluOpType
AX = mybir.AxisListType

H = 4096
CH = 32
TOK = 1024
TT = 512
NTB = 2
KVH = 8
QH = 32
IT = 86
EPS_LN = 1.0e-5
EPS_ATT = 1.0e-10
CCW = KVH * 128 + KVH

ENGS = ("sync", "gpsimd", "tensor", "scalar", "vector")


class Prog:
    def __init__(self):
        self.q = {e: [] for e in ENGS}
        self.hi = {e: {} for e in ENGS}

    def add(self, eng, fn):
        self.q[eng].append(fn)

    def wait(self, eng, sem, val):
        if val <= 0:
            return
        key = id(sem)
        if self.hi[eng].get(key, -1) >= val:
            return
        self.hi[eng][key] = val
        self.q[eng].append(lambda e, s=sem, v=val: e.wait_ge(s, v))

    def emit(self, block):
        for name in ENGS:
            ops = self.q[name]
            if not ops:
                continue
            getattr(block, name)(lambda e, ops=ops: [f(e) for f in ops])


class Ctr:
    def __init__(self):
        self.v = 0

    def inc(self, n=1):
        self.v += n
        return self.v


def build(upto="b4", dumps=()):
    nc = bass.Bass("TRN2", target_bir_lowering=False, debug=False)

    xT = nc.dram_tensor("xT", [H, TOK], F32, kind="ExternalInput").ap()
    wimg = {}
    for nm, nblk, wide in (("wq", QH, H), ("wk", KVH, H), ("wv", KVH, H),
                           ("wo", CH, H), ("wg", IT, H), ("wu", IT, H),
                           ("wd", CH, IT * 128)):
        wimg[nm] = nc.dram_tensor(nm, [nblk, 128, wide], F32, kind="ExternalInput").ap()
    bvecs = {}
    for nm, w in (("bq", QH), ("bk", KVH), ("bv", KVH), ("bo", CH),
                  ("ag", IT), ("au", IT), ("ad", CH)):
        bvecs[nm] = nc.dram_tensor(nm, [128, w], F32, kind="ExternalInput").ap()
    ident_in = nc.dram_tensor("ident", [128, 128], BF16, kind="ExternalInput").ap()
    outT = nc.dram_tensor("outT", [H, TOK], F32, kind="ExternalOutput").ap()
    x2d = nc.dram_tensor("x2d", [H, TOK], F32, kind="Internal").ap()
    cc_in = nc.dram_tensor("cc_in", [128, CCW], F32, kind="Internal").ap()
    cc_out = nc.dram_tensor("cc_out", [128, CCW], F32, kind="Internal").ap()

    sem_ms = nc.alloc_semaphore("s_ms")
    sem_vs = nc.alloc_semaphore("s_vs")
    sem_ss = nc.alloc_semaphore("s_ss")
    sem_cs = nc.alloc_semaphore("s_cs")
    MS, VS, SS = Ctr(), Ctr(), Ctr()
    semG = [nc.alloc_semaphore(f"s_g{i}") for i in range(4)]
    ctrG = [Ctr() for _ in range(4)]
    semX = [nc.alloc_semaphore(f"s_x{i}") for i in range(4)]
    ctrX = [Ctr() for _ in range(4)]
    semO = [nc.alloc_semaphore(f"s_o{i}") for i in range(2)]
    ctrO = [Ctr() for _ in range(2)]
    semA = [nc.alloc_semaphore(f"s_a{i}") for i in range(4)]
    ctrA = [Ctr() for _ in range(4)]

    ident_sb = nc.alloc_sbuf_tensor("ident_sb", [128, 128], BF16)
    ones_col = nc.alloc_sbuf_tensor("ones_col", [128, 16], BF16)
    ones_row = nc.alloc_sbuf_tensor("ones_row", [1, 128], BF16)
    b_sb = {nm: nc.alloc_sbuf_tensor(nm + "_sb", [128, w], F32)
            for nm, w in (("bq", QH), ("bk", KVH), ("bv", KVH), ("bo", CH),
                          ("ag", IT), ("au", IT), ("ad", CH))}
    ln2a = [nc.alloc_sbuf_tensor(f"ln2a{t}", [1, TT], BF16) for t in range(NTB)]
    ln2b = [nc.alloc_sbuf_tensor(f"ln2b{t}", [1, TT], BF16) for t in range(NTB)]
    sc_t = nc.alloc_sbuf_tensor("sc_t", [128, 2 * TT], F32)
    sc_e = nc.alloc_sbuf_tensor("sc_e", [128, 2 * TT], BF16)
    sc_r = nc.alloc_sbuf_tensor("sc_r", [128, 2 * TT], BF16)
    sc_bf = nc.alloc_sbuf_tensor("sc_bf", [128, 2 * TT], BF16)
    lnA_sb = nc.alloc_sbuf_tensor("lnA_sb", [128, TT], BF16)
    lnB_sb = nc.alloc_sbuf_tensor("lnB_sb", [128, TT], BF16)
    sc_o = nc.alloc_sbuf_tensor("sc_o", [128, 2 * TT], F32)
    xre = nc.alloc_sbuf_tensor("xre", [128, 4 * TT], F32)
    wstream = nc.alloc_sbuf_tensor("wstream", [128, IT * 128], BF16)
    kvt_bf = nc.alloc_sbuf_tensor("kvt_bf", [128, KVH * 128], BF16)
    rn_sb = nc.alloc_sbuf_tensor("rn_sb", [128, KVH], F32)
    rn_tmp = nc.alloc_sbuf_tensor("rn_tmp", [128, KVH], F32)

    TOP = 229344
    MB_BYTES = 88064
    MA_BYTES = 65536
    MB0 = (TOP - MB_BYTES) // 32 * 32
    MA0 = (MB0 - MA_BYTES) // 32 * 32
    A = lambda name, shape, dt, off: nc.alloc_sbuf_tensor_at(name, shape, dt, offset=off)
    xln1 = A("xln1", [128, CH * TOK], BF16, MA0)
    xln2 = A("xln2", [128, CH * TT], BF16, MA0)
    gu = A("gu", [128, 4 * H], BF16, MA0 + 32768)
    xbf_full = A("xbf_full", [128, CH * TT], BF16, MB0)
    xsq_r = A("xsq_r", [128, 4 * TT], BF16, MB0 + 32768)
    kphi = A("kphi", [128, KVH * TOK], BF16, MB0)
    vbf = A("vbf", [128, KVH * TOK], BF16, MB0 + 16384)
    kt_r = A("kt_r", [128, 4 * 128], BF16, MB0 + 32768)
    vt_r = A("vt_r", [128, 4 * 128], BF16, MB0 + 33792)
    qphi = A("qphi", [128, QH * TOK], BF16, MB0)
    h_sb = A("h_sb", [128, IT * TT], BF16, MB0)
    row_mean = A("row_mean", [1, TT], F32, MB0 + 65536)
    row_var = A("row_var", [1, TT], F32, MB0 + 67584)
    row_r = A("row_r", [1, TT], F32, MB0 + 69632)
    row_a_bf = A("row_a_bf", [1, TT], BF16, MB0 + 71680)
    row_b_bf = A("row_b_bf", [1, TT], BF16, MB0 + 72704)
    ccin_sb = A("ccin_sb", [128, CCW], F32, MB0 + 73728)
    ccout_sb = A("ccout_sb", [128, CCW], F32, MB0 + 77888)
    x2a = A("x2a", [128, 4 * TT], BF16, MB0 + 65536)
    x2bf = A("x2bf", [128, 2 * TT], BF16, MB0 + 82048)
    x2sq = A("x2sq", [128, TT], BF16, MB0 + 84096)
    row_b = A("row_b", [1, TT], F32, MB0 + 85120)

    def sl(t, i, w):
        return t[:, i * w:(i + 1) * w]

    free = {}
    ready = {}
    xre_ctr = Ctr()

    def dma(P, eng, sems, ctrs, idx, dst, src):
        P.add(eng, lambda e, d=dst, s=src, sm=sems[idx]: e.dma_start(d, s).then_inc(sm, 16))
        return (sems[idx], ctrs[idx].inc(16))

    P = Prog()
    _b0_target = ctrX[0].v + 8 * 16
    def b0_sync(sy, v_target=_b0_target):
        sy.dma_start(ident_sb[:], ident_in[:]).then_inc(semX[0], 16)
        for nm in ("bq", "bk", "bv", "bo", "ag", "au", "ad"):
            sy.dma_start(b_sb[nm][:], bvecs[nm][:]).then_inc(semX[0], 16)
        sy.wait_ge(semX[0], v_target)
    P.add("sync", b0_sync)
    ctrX[0].inc(8 * 16)
    P.add("vector", lambda v: v.memset(ones_col[:], 1.0))
    P.add("vector", lambda v: v.memset(ones_row[:], 1.0))
    with nc.Block() as b0:
        P.emit(b0)

    def ln_finalize(P, s1_ps, s2_ps, a_dst, b_dst, stats_done_ms):
        inv = 1.0 / H
        P.wait("vector", sem_ms, stats_done_ms)
        P.add("vector", lambda v: v.tensor_scalar(row_mean[:], s1_ps[:], inv, None, ALU.mult))
        P.add("vector", lambda v: v.tensor_scalar(row_var[:], s2_ps[:], inv, None, ALU.mult)
              .then_inc(sem_vs, 1))
        v1 = VS.inc()
        P.wait("vector", sem_vs, v1)
        P.add("vector", lambda v: v.tensor_tensor(row_r[:], row_mean[:], row_mean[:], ALU.mult)
              .then_inc(sem_vs, 1))
        v2 = VS.inc()
        P.wait("vector", sem_vs, v2)
        P.add("vector", lambda v: v.tensor_tensor(row_var[:], row_var[:], row_r[:], ALU.subtract)
              .then_inc(sem_vs, 1))
        v_var = VS.inc()
        P.wait("scalar", sem_vs, v_var)
        P.add("scalar", lambda s: s.activation(row_b[:], row_var[:], AF.Ln,
                                               bias=np.float32(EPS_LN), scale=1.0)
              .then_inc(sem_ss, 1))
        s_ln = SS.inc()
        P.wait("scalar", sem_ss, s_ln)
        P.add("scalar", lambda s: s.activation(row_r[:], row_b[:], AF.Exp, bias=0.0, scale=-0.5)
              .then_inc(sem_ss, 1))
        s_r = SS.inc()
        P.wait("vector", sem_ss, s_r)
        P.add("vector", lambda v: v.scalar_tensor_tensor(row_b[:], row_mean[:], -1.0, row_r[:],
                                                         ALU.mult, ALU.mult).then_inc(sem_vs, 1))
        v4 = VS.inc()
        P.add("vector", lambda v, a=a_dst: v.tensor_copy(a[:], row_r[:]))
        P.wait("vector", sem_vs, v4)
        P.add("vector", lambda v, b=b_dst: v.tensor_copy(b[:], row_b[:]).then_inc(sem_vs, 1))
        return VS.inc()

    P = Prog()
    with (
        nc.psum_tensor("ps_s1", [1, TT], F32) as ps_s1,
        nc.psum_tensor("ps_s2", [1, TT], F32) as ps_s2,
        nc.psum_tensor("ps_abc", [128, TT], F32) as ps_abc,
        nc.psum_tensor("ps_bbc", [128, TT], F32) as ps_bbc,
    ):
        sq_vals = {}
        for tb in range(NTB):
            for c in range(CH):
                gsl = c % 4
                if tb * CH + c >= 4:
                    P.wait("gpsimd", sem_ss, sq_vals[tb * CH + c - 4])
                if ("xbf", c) in free:
                    P.wait("gpsimd", *free[("xbf", c)])
                w_xbf = dma(P, "gpsimd", semG, ctrG, gsl, sl(xbf_full, c, TT),
                            xT[c * 128:(c + 1) * 128, tb * TT:(tb + 1) * TT])
                P.wait("scalar", *w_xbf)
                if ("xsq", gsl) in free:
                    P.wait("scalar", *free[("xsq", gsl)])
                P.add("scalar", lambda s, i=gsl, c=c:
                      s.activation(sl(xsq_r, i, TT), sl(xbf_full, c, TT), AF.Square)
                      .then_inc(sem_ss, 1))
                s_sq = SS.inc()
                sq_vals[tb * CH + c] = s_sq
                P.wait("tensor", *w_xbf)
                P.wait("tensor", sem_ss, s_sq)
                if tb == 1 and c == 0:
                    P.wait("tensor", sem_vs, ready[("lnfin1", 0)])
                P.add("tensor", lambda t, c=c, st=(c == 0), sp=(c == CH - 1):
                      t.matmul(ps_s1[:], ones_col[:, 0:1], sl(xbf_full, c, TT), start=st, stop=sp))
                P.add("tensor", lambda t, i=gsl, st=(c == 0), sp=(c == CH - 1):
                      t.matmul(ps_s2[:], ones_col[:, 0:1], sl(xsq_r, i, TT), start=st, stop=sp)
                      .then_inc(sem_ms, 1))
                m_chunk = MS.inc()
                free[("xsq", gsl)] = (sem_ms, m_chunk)
            ready[("lnfin1", tb)] = VS.v + 1
            rows_ready = ln_finalize(P, ps_s1, ps_s2, row_a_bf, row_b_bf, m_chunk)
            P.wait("tensor", sem_vs, rows_ready)
            if tb == 1:
                P.wait("tensor", sem_vs, ready[("applydone", 0)])
            P.add("tensor", lambda t: t.matmul(ps_abc[:], ones_row[:], row_a_bf[:],
                                               start=True, stop=True))
            P.add("tensor", lambda t: t.matmul(ps_bbc[:], ones_row[:], row_b_bf[:],
                                               start=True, stop=True).then_inc(sem_ms, 1))
            m_bc = MS.inc()
            P.wait("vector", sem_ms, m_bc)
            if tb == 1:
                P.wait("vector", sem_vs, ready[("applydone", 0)])
            P.add("vector", lambda v: v.tensor_copy(lnA_sb[:], ps_abc[:]))
            P.add("vector", lambda v: v.tensor_copy(lnB_sb[:], ps_bbc[:]).then_inc(sem_vs, 1))
            v_ab = VS.inc()
            P.wait("vector", sem_vs, v_ab)
            for c in range(CH):
                P.add("vector", lambda v, c=c:
                      v.tensor_tensor(sl(sc_bf, c % 2, TT), sl(xbf_full, c, TT), lnA_sb[:], ALU.mult)
                      .then_inc(sem_vs, 1))
                v_t = VS.inc()
                P.wait("vector", sem_vs, v_t)
                P.add("vector", lambda v, c=c, tb=tb:
                      v.tensor_tensor(xln1[:, c * TOK + tb * TT: c * TOK + (tb + 1) * TT],
                                      sl(sc_bf, c % 2, TT), lnB_sb[:], ALU.add)
                      .then_inc(sem_vs, 1))
                v_ap = VS.inc()
                free[("xbf", c)] = (sem_vs, v_ap)
            ready[("applydone", tb)] = v_ap
        with nc.Block() as b1:
            P.emit(b1)

    if upto == "b1":
        return nc, _emit_dumps(nc, dumps, locals())

    tile = Ctr()
    widx = Ctr()

    def w_block_dma(P, img, j, width=H):
        s = widx.v % 2
        if ("wslot", s) in free:
            P.wait("gpsimd", *free[("wslot", s)])
        pair = dma(P, "gpsimd", semG, ctrG, s,
                   wstream[:, s * H:s * H + width], img[j])
        return pair, s

    def proj_block(P, ps_ring, img, j, bias_col, kind, out_sb, out_off, ring_n=4):
        wv, ws_slot = w_block_dma(P, img, j)
        widx.inc()
        for tb in range(NTB):
            t = tile.v
            tile.inc()
            slot = t % ring_n
            P.wait("tensor", *wv)
            if ("ps", slot) in free:
                P.wait("tensor", *free[("ps", slot)])
            for c in range(CH):
                st, sp = (c == 0), (c == CH - 1)
                if sp:
                    P.add("tensor", lambda te, c=c, s=slot, j2=ws_slot, tb=tb:
                          te.matmul(sl(ps_ring, s, TT),
                                    wstream[:, j2 * H + c * 128: j2 * H + (c + 1) * 128],
                                    xln1[:, c * TOK + tb * TT: c * TOK + (tb + 1) * TT],
                                    start=False, stop=True).then_inc(sem_ms, 1))
                else:
                    P.add("tensor", lambda te, c=c, s=slot, j2=ws_slot, tb=tb, st=st:
                          te.matmul(sl(ps_ring, s, TT),
                                    wstream[:, j2 * H + c * 128: j2 * H + (c + 1) * 128],
                                    xln1[:, c * TOK + tb * TT: c * TOK + (tb + 1) * TT],
                                    start=st, stop=False))
            m_grp = MS.inc()
            dst = out_sb[:, out_off + tb * TT: out_off + (tb + 1) * TT]
            if kind == "phi":
                P.wait("vector", sem_ms, m_grp)
                if ("sct", t % 2) in free:
                    P.wait("vector", *free[("sct", t % 2)])
                P.add("vector", lambda v, s=slot, i=t % 2, b=bias_col:
                      v.tensor_scalar(sl(sc_t, i, TT), sl(ps_ring, s, TT), b, 0.0,
                                      ALU.add, ALU.min))
                P.add("vector", lambda v, s=slot, i=t % 2, b=bias_col:
                      v.tensor_scalar(sl(sc_r, i, TT), sl(ps_ring, s, TT), b, 0.0,
                                      ALU.add, ALU.max).then_inc(sem_vs, 1))
                v_mm = VS.inc()
                P.wait("scalar", sem_vs, v_mm)
                if ("sce", t % 2) in free:
                    P.wait("scalar", *free[("sce", t % 2)])
                P.add("scalar", lambda s, i=t % 2:
                      s.activation(sl(sc_e, i, TT), sl(sc_t, i, TT), AF.Exp).then_inc(sem_ss, 1))
                s_exp = SS.inc()
                free[("sct", t % 2)] = (sem_ss, s_exp)
                P.wait("vector", sem_ss, s_exp)
                P.wait("vector", sem_vs, v_mm)
                P.add("vector", lambda v, i=t % 2, d=dst:
                      v.tensor_tensor(d, sl(sc_e, i, TT), sl(sc_r, i, TT), ALU.add)
                      .then_inc(sem_vs, 1))
                v_done = VS.inc()
                free[("sce", t % 2)] = (sem_vs, v_done)
                free[("ps", slot)] = (sem_vs, v_done)
                ready[("tile", t)] = (sem_vs, v_done)
            else:
                P.wait("scalar", sem_ms, m_grp)
                P.add("scalar", lambda s, sp2=slot, b=bias_col, d=dst:
                      s.activation(d, sl(ps_ring, sp2, TT), AF.Identity, bias=b, scale=1.0)
                      .then_inc(sem_ss, 1))
                s_done = SS.inc()
                free[("ps", slot)] = (sem_ss, s_done)
                ready[("tile", t)] = (sem_ss, s_done)
        free[("wslot", ws_slot)] = (sem_ms, m_grp)

    P = Prog()
    with (
        nc.psum_tensor("ps_ringA", [128, 2 * TT], F32) as ps_ringA,
        nc.psum_tensor("ps_tpA0", [128, 128], BF16) as ps_tpA0,
        nc.psum_tensor("ps_tpA1", [128, 128], BF16) as ps_tpA1,
        nc.psum_tensor("ps_tpB0", [128, 128], BF16) as ps_tpB0,
        nc.psum_tensor("ps_tpB1", [128, 128], BF16) as ps_tpB1,
        nc.psum_tensor("ps_kvA", [128, 128], F32) as ps_kvA,
        nc.psum_tensor("ps_kvB", [128, 128], F32) as ps_kvB,
    ):
        k_tiles = []
        for j in range(KVH):
            proj_block(P, ps_ringA, wimg["wk"], j, b_sb["bk"][:, j:j + 1], "phi", kphi,
                       j * TOK, ring_n=2)
            k_tiles += [tile.v - 2, tile.v - 1]
        v_tiles = []
        for j in range(KVH):
            proj_block(P, ps_ringA, wimg["wv"], j, b_sb["bv"][:, j:j + 1], "id", vbf,
                       j * TOK, ring_n=2)
            v_tiles += [tile.v - 2, tile.v - 1]
        P.wait("tensor", *ready[("tile", k_tiles[-1])])
        P.wait("tensor", *ready[("tile", v_tiles[-1])])
        kv_ps = [ps_kvA, ps_kvB]
        tpA = [ps_tpA0, ps_tpA1]
        tpB = [ps_tpB0, ps_tpB1]
        mkv = {}
        tpcopy = {}

        def emit_transposes(i):
            hh, c = i // 8, i % 8
            par = i % 2
            if ("tpA", par) in free:
                P.wait("tensor", *free[("tpA", par)])
            P.add("tensor", lambda te, hh=hh, c=c, par=par:
                  te.transpose(tpA[par][:],
                               kphi[:, hh * TOK + c * 128: hh * TOK + (c + 1) * 128],
                               ident_sb[:]).then_inc(sem_ms, 1))
            m_tk = MS.inc()
            P.add("tensor", lambda te, hh=hh, c=c, par=par:
                  te.transpose(tpB[par][:],
                               vbf[:, hh * TOK + c * 128: hh * TOK + (c + 1) * 128],
                               ident_sb[:]).then_inc(sem_ms, 1))
            m_tv = MS.inc()
            P.wait("vector", sem_ms, m_tk)
            if i >= 4:
                P.wait("vector", sem_ms, mkv.get(i - 4, 0))
            P.add("vector", lambda v, i=i, par=par: v.tensor_copy(sl(kt_r, i % 4, 128), tpA[par][:]))
            P.wait("vector", sem_ms, m_tv)
            P.add("vector", lambda v, i=i, par=par: v.tensor_copy(sl(vt_r, i % 4, 128), tpB[par][:])
                  .then_inc(sem_vs, 1))
            tpcopy[i] = VS.inc()
            free[("tpA", par)] = (sem_vs, tpcopy[i])

        def emit_kv_mm(i):
            hh, c = i // 8, i % 8
            P.wait("tensor", sem_vs, tpcopy[i])
            if c == 0 and ("kvps", hh % 2) in free:
                P.wait("tensor", *free[("kvps", hh % 2)])
            st, sp = (c == 0), (c == 7)
            P.add("tensor", lambda te, hh=hh, i=i, st=st, sp=sp:
                  te.matmul(kv_ps[hh % 2][:], sl(kt_r, i % 4, 128), sl(vt_r, i % 4, 128),
                            start=st, stop=sp).then_inc(sem_ms, 1))
            mkv[i] = MS.inc()
            if sp:
                P.wait("vector", sem_ms, mkv[i])
                P.add("vector", lambda v, hh=hh:
                      v.tensor_copy(ccin_sb[:, hh * 128:(hh + 1) * 128], kv_ps[hh % 2][:])
                      .then_inc(sem_vs, 1))
                free[("kvps", hh % 2)] = (sem_vs, VS.inc())

        emit_transposes(0)
        for i in range(1, KVH * 8):
            emit_transposes(i)
            emit_kv_mm(i - 1)
        emit_kv_mm(KVH * 8 - 1)
        for hh in range(KVH):
            if hh == KVH - 1:
                P.add("vector", lambda v, hh=hh:
                      v.reduce_sum(ccin_sb[:, KVH * 128 + hh: KVH * 128 + hh + 1],
                                   kphi[:, hh * TOK:(hh + 1) * TOK], axis=AX.X)
                      .then_inc(sem_vs, 1))
            else:
                P.add("vector", lambda v, hh=hh:
                      v.reduce_sum(ccin_sb[:, KVH * 128 + hh: KVH * 128 + hh + 1],
                                   kphi[:, hh * TOK:(hh + 1) * TOK], axis=AX.X))
        v_stage = VS.inc()
        P.wait("sync", sem_vs, v_stage)
        o_stage = dma(P, "sync", semO, ctrO, 0, cc_in[:], ccin_sb[:])
        P.wait("gpsimd", *o_stage)
        P.add("gpsimd", lambda g: g.collective_compute(
            "AllReduce", ALU.add, replica_groups=[[0, 1, 2, 3], [4, 5, 6, 7]],
            ins=[cc_in[:]], outs=[cc_out[:]]).then_inc(sem_cs, 1))
        with nc.Block() as b2a:
            P.emit(b2a)

    if upto == "b2a":
        return nc, _emit_dumps(nc, dumps, locals())

    P = Prog()
    with nc.psum_tensor("ps_ringB", [128, 4 * TT], F32) as ps_ringB:
        for k in range(4):
            free.pop(("ps", k), None)
        P.wait("sync", sem_cs, 1)
        x_cc = dma(P, "sync", semX, ctrX, 0, ccout_sb[:], cc_out[:])
        q_tiles = []
        post = [False]
        for j in range(QH):
            proj_block(P, ps_ringB, wimg["wq"], j, b_sb["bq"][:, j:j + 1], "phi", qphi, j * TOK)
            q_tiles += [tile.v - 2, tile.v - 1]
            if j == 10 and not post[0]:
                P.wait("vector", *x_cc)
                P.add("vector", lambda v: v.tensor_copy(kvt_bf[:], ccout_sb[:, :KVH * 128]))
                P.add("vector", lambda v: v.tensor_scalar(rn_tmp[:], ccout_sb[:, KVH * 128:],
                                                          EPS_ATT, None, ALU.add)
                      .then_inc(sem_vs, 1))
                v_rt = VS.inc()
                P.wait("vector", sem_vs, v_rt)
                P.add("vector", lambda v: v.reciprocal(rn_sb[:], rn_tmp[:]).then_inc(sem_vs, 1))
                ready[("kvrdy",)] = (sem_vs, VS.inc())
                post[0] = True
        P.wait("tensor", *ready[("kvrdy",)])
        for qh in range(QH):
            kvh = qh // 4
            for tb in range(NTB):
                t = tile.v
                tile.inc()
                slot = t % 4
                P.wait("tensor", *ready[("tile", q_tiles[qh * 2 + tb])])
                if ("ps", slot) in free:
                    P.wait("tensor", *free[("ps", slot)])
                P.add("tensor", lambda te, qh=qh, kvh=kvh, tb=tb, s=slot:
                      te.matmul(sl(ps_ringB, s, TT), kvt_bf[:, kvh * 128:(kvh + 1) * 128],
                                qphi[:, qh * TOK + tb * TT: qh * TOK + (tb + 1) * TT],
                                start=True, stop=True).then_inc(sem_ms, 1))
                m_at = MS.inc()
                P.wait("scalar", *ready[("kvrdy",)])
                P.wait("scalar", sem_ms, m_at)
                P.add("scalar", lambda s, qh=qh, kvh=kvh, tb=tb, sp=slot:
                      s.activation(qphi[:, qh * TOK + tb * TT: qh * TOK + (tb + 1) * TT],
                                   sl(ps_ringB, sp, TT), AF.Identity,
                                   bias=0.0, scale=rn_sb[:, kvh:kvh + 1]).then_inc(sem_ss, 1))
                s_at = SS.inc()
                free[("ps", slot)] = (sem_ss, s_at)
                ready[("attn", qh * 2 + tb)] = (sem_ss, s_at)
        with nc.Block() as b2b:
            P.emit(b2b)

    if upto == "b2b":
        return nc, _emit_dumps(nc, dumps, locals())

    P = Prog()
    with (
        nc.psum_tensor("ps_ringC", [128, 4 * TT], F32) as ps_ringC,
        nc.psum_tensor("ps2_s1a", [1, TT], F32) as ps2_s1a,
        nc.psum_tensor("ps2_s2a", [1, TT], F32) as ps2_s2a,
        nc.psum_tensor("ps2_s1b", [1, TT], F32) as ps2_s1b,
        nc.psum_tensor("ps2_s2b", [1, TT], F32) as ps2_s2b,
    ):
        for k in range(4):
            free.pop(("ps", k), None)
        P.wait("tensor", *ready[("attn", 63)])
        st_ps = [(ps2_s1a, ps2_s2a), (ps2_s1b, ps2_s2b)]
        x2bf_rdy, x2sq_rdy = {}, {}
        for s in range(CH * NTB):
            j, tb = s // 2, s % 2
            wv, ws_slot = w_block_dma(P, wimg["wo"], j)
            widx.inc()
            t = tile.v
            tile.inc()
            slot = t % 4
            P.wait("tensor", *wv)
            if ("ps", slot) in free:
                P.wait("tensor", *free[("ps", slot)])
            for c in range(CH):
                st, sp2 = (c == 0), (c == CH - 1)
                if sp2:
                    P.add("tensor", lambda te, c=c, s2=slot, j2=ws_slot, tb=tb:
                          te.matmul(sl(ps_ringC, s2, TT),
                                    wstream[:, j2 * H + c * 128: j2 * H + (c + 1) * 128],
                                    qphi[:, c * TOK + tb * TT: c * TOK + (tb + 1) * TT],
                                    start=False, stop=True).then_inc(sem_ms, 1))
                else:
                    P.add("tensor", lambda te, c=c, s2=slot, j2=ws_slot, tb=tb, st=st:
                          te.matmul(sl(ps_ringC, s2, TT),
                                    wstream[:, j2 * H + c * 128: j2 * H + (c + 1) * 128],
                                    qphi[:, c * TOK + tb * TT: c * TOK + (tb + 1) * TT],
                                    start=st, stop=False))
            m_grp = MS.inc()
            free[("wslot", ws_slot)] = (sem_ms, m_grp)
            if s >= 1:
                sp_, jp, tbp = s - 1, (s - 1) // 2, (s - 1) % 2
                P.wait("tensor", sem_vs, x2bf_rdy[sp_])
                P.wait("tensor", sem_ss, x2sq_rdy[sp_])
                P.add("tensor", lambda te, tbp=tbp, i=sp_ % 2, st=(jp == 0), sp2=(jp == CH - 1):
                      te.matmul(st_ps[tbp][0][:], ones_col[:, 0:1], sl(x2bf, i, TT),
                                start=st, stop=sp2))
                P.add("tensor", lambda te, tbp=tbp, st=(jp == 0), sp2=(jp == CH - 1):
                      te.matmul(st_ps[tbp][1][:], ones_col[:, 0:1], x2sq[:],
                                start=st, stop=sp2).then_inc(sem_ms, 1))
                ready[("x2stat", sp_)] = MS.inc()
            xsl = xre_ctr.v % 4
            xre_ctr.inc()
            if ("xre", xsl) in free:
                P.wait("sync", *free[("xre", xsl)])
            x_ld = dma(P, "sync", semX, ctrX, xsl, sl(xre, xsl, TT),
                       xT[j * 128:(j + 1) * 128, tb * TT:(tb + 1) * TT])
            P.wait("scalar", sem_ms, m_grp)
            if ("sct", s % 2) in free:
                P.wait("scalar", *free[("sct", s % 2)])
            P.add("scalar", lambda sc, s2=slot, j=j, i=s % 2:
                  sc.activation(sl(sc_t, i, TT), sl(ps_ringC, s2, TT), AF.Identity,
                                bias=b_sb["bo"][:, j:j + 1], scale=1.0).then_inc(sem_ss, 1))
            s_id = SS.inc()
            free[("ps", slot)] = (sem_ss, s_id)
            P.wait("vector", sem_ss, s_id)
            P.wait("vector", *x_ld)
            if ("sco", s % 2) in free:
                P.wait("vector", *free[("sco", s % 2)])
            P.add("vector", lambda v, i=s % 2, xs=xsl:
                  v.tensor_tensor(sl(sc_o, i, TT), sl(sc_t, i, TT), sl(xre, xs, TT), ALU.add)
                  .then_inc(sem_vs, 1))
            v_x2 = VS.inc()
            free[("sct", s % 2)] = (sem_vs, v_x2)
            free[("xre", xsl)] = (sem_vs, v_x2)
            P.wait("scalar", sem_vs, v_x2)
            if s >= 1:
                P.wait("scalar", sem_ms, ready[("x2stat", s - 1)])
            P.add("scalar", lambda sc, i=s % 2:
                  sc.activation(x2sq[:], sl(sc_o, i, TT), AF.Square).then_inc(sem_ss, 1))
            x2sq_rdy[s] = SS.inc()
            if s >= 2:
                P.wait("vector", sem_ms, ready[("x2stat", s - 2)])
            P.wait("vector", sem_vs, v_x2)
            P.add("vector", lambda v, i=s % 2:
                  v.tensor_copy(sl(x2bf, i, TT), sl(sc_o, i, TT)).then_inc(sem_vs, 1))
            x2bf_rdy[s] = VS.inc()
            P.wait("sync", sem_vs, v_x2)
            o_x2 = dma(P, "sync", semO, ctrO, s % 2,
                       x2d[j * 128:(j + 1) * 128, tb * TT:(tb + 1) * TT], sl(sc_o, i := s % 2, TT))
            free[("sco", s % 2)] = o_x2
        sp_ = CH * NTB - 1
        tbp = sp_ % 2
        P.wait("tensor", sem_vs, x2bf_rdy[sp_])
        P.wait("tensor", sem_ss, x2sq_rdy[sp_])
        P.add("tensor", lambda te, tbp=tbp, i=sp_ % 2:
              te.matmul(st_ps[tbp][0][:], ones_col[:, 0:1], sl(x2bf, i, TT),
                        start=False, stop=True))
        P.add("tensor", lambda te, tbp=tbp:
              te.matmul(st_ps[tbp][1][:], ones_col[:, 0:1], x2sq[:],
                        start=False, stop=True).then_inc(sem_ms, 1))
        m_last = MS.inc()

        def ln2_fin(P, tb, stats_ms):
            inv = 1.0 / H
            P.wait("vector", sem_ms, stats_ms)
            P.add("vector", lambda v, tb=tb:
                  v.tensor_scalar(row_mean[:], st_ps[tb][0][:], inv, None, ALU.mult))
            P.add("vector", lambda v, tb=tb:
                  v.tensor_scalar(row_var[:], st_ps[tb][1][:], inv, None, ALU.mult)
                  .then_inc(sem_vs, 1))
            v1 = VS.inc()
            P.wait("vector", sem_vs, v1)
            P.add("vector", lambda v: v.tensor_tensor(row_r[:], row_mean[:], row_mean[:], ALU.mult)
                  .then_inc(sem_vs, 1))
            v2 = VS.inc()
            P.wait("vector", sem_vs, v2)
            P.add("vector", lambda v: v.tensor_tensor(row_var[:], row_var[:], row_r[:], ALU.subtract)
                  .then_inc(sem_vs, 1))
            v_var = VS.inc()
            P.wait("scalar", sem_vs, v_var)
            P.add("scalar", lambda sc: sc.activation(row_b[:], row_var[:], AF.Ln,
                                                     bias=np.float32(EPS_LN), scale=1.0)
                  .then_inc(sem_ss, 1))
            s_ln = SS.inc()
            P.wait("scalar", sem_ss, s_ln)
            P.add("scalar", lambda sc: sc.activation(row_r[:], row_b[:], AF.Exp, bias=0.0, scale=-0.5)
                  .then_inc(sem_ss, 1))
            s_r = SS.inc()
            P.wait("vector", sem_ss, s_r)
            P.add("vector", lambda v: v.scalar_tensor_tensor(row_b[:], row_mean[:], -1.0, row_r[:],
                                                             ALU.mult, ALU.mult).then_inc(sem_vs, 1))
            v4 = VS.inc()
            P.add("vector", lambda v, tb=tb: v.tensor_copy(ln2a[tb][:], row_r[:]))
            P.wait("vector", sem_vs, v4)
            P.add("vector", lambda v, tb=tb: v.tensor_copy(ln2b[tb][:], row_b[:]).then_inc(sem_vs, 1))
            return VS.inc()

        ln2_fin(P, 0, ready[("x2stat", CH * NTB - 2)])
        ln2_fin(P, 1, m_last)
        P.add("sync", lambda sy, v0=ctrO[0].v, v1=ctrO[1].v:
              (sy.wait_ge(semO[0], v0), sy.wait_ge(semO[1], v1)))
        with nc.Block() as b3:
            P.emit(b3)

    if upto == "b3":
        return nc, _emit_dumps(nc, dumps, locals())

    P = Prog()
    with (
        nc.psum_tensor("ps_a2", [128, TT], F32) as ps_a2,
        nc.psum_tensor("ps_b2", [128, TT], F32) as ps_b2,
        nc.psum_tensor("ps_ringD", [128, 4 * TT], F32) as ps_ringD,
    ):
        for k in range(4):
            free.pop(("ps", k), None)
        free.pop(("wslot", 0), None)
        free.pop(("wslot", 1), None)
        gtile = Ctr()
        last_up, last_dn = {}, {}
        for tb in range(NTB):
            if tb == 1:
                P.wait("tensor", sem_vs, ready[("ffnapply", 0)])
            P.add("tensor", lambda te, tb=tb: te.matmul(ps_a2[:], ones_row[:], ln2a[tb][:],
                                                        start=True, stop=True))
            P.add("tensor", lambda te, tb=tb: te.matmul(ps_b2[:], ones_row[:], ln2b[tb][:],
                                                        start=True, stop=True).then_inc(sem_ms, 1))
            m_bc = MS.inc()
            guw_pre = {}
            for jt0 in range(2):
                for st_i, img in ((0, wimg["wg"]), (1, wimg["wu"])):
                    gslot = st_i * 2 + (jt0 % 2)
                    if ("gu", gslot) in free:
                        P.wait("gpsimd", *free[("gu", gslot)])
                    if ("wd", gslot) in free:
                        P.wait("gpsimd", *free[("wd", gslot)])
                    guw_pre[(jt0, st_i)] = dma(P, "gpsimd", semG, ctrG, gslot,
                                               sl(gu, gslot, H), img[jt0])
            P.wait("vector", sem_ms, m_bc)
            if tb == 1:
                P.wait("vector", sem_ms, last_up[0])
            P.add("vector", lambda v: v.tensor_copy(lnA_sb[:], ps_a2[:]))
            P.add("vector", lambda v: v.tensor_copy(lnB_sb[:], ps_b2[:]).then_inc(sem_vs, 1))
            v_ab = VS.inc()
            P.wait("vector", sem_vs, v_ab)
            apply_v = {}
            for c in range(CH):
                asl = c % 4
                if ("x2a", asl) in free:
                    P.wait("gpsimd", *free[("x2a", asl)])
                x_ld = dma(P, "gpsimd", semA, ctrA, asl, sl(x2a, asl, TT),
                           x2d[c * 128:(c + 1) * 128, tb * TT:(tb + 1) * TT])
                P.wait("vector", *x_ld)
                P.add("vector", lambda v, i=asl, c=c:
                      v.tensor_tensor(sl(sc_bf, c % 2, TT), sl(x2a, i, TT), lnA_sb[:], ALU.mult)
                      .then_inc(sem_vs, 1))
                v_t = VS.inc()
                P.wait("vector", sem_vs, v_t)
                P.add("vector", lambda v, c=c:
                      v.tensor_tensor(xln2[:, c * TT:(c + 1) * TT], sl(sc_bf, c % 2, TT),
                                      lnB_sb[:], ALU.add).then_inc(sem_vs, 1))
                v_ap = VS.inc()
                apply_v[c] = v_ap
                free[("x2a", asl)] = (sem_vs, v_ap)
            ready[("ffnapply", tb)] = apply_v[CH - 1]
            h_rdy = {}
            for jt in range(IT):
                guw = {}
                for st_i, img in ((0, wimg["wg"]), (1, wimg["wu"])):
                    if jt < 2:
                        guw[st_i] = guw_pre[(jt, st_i)]
                        continue
                    gslot = st_i * 2 + (jt % 2)
                    if ("gu", gslot) in free:
                        P.wait("gpsimd", *free[("gu", gslot)])
                    if ("wd", gslot) in free:
                        P.wait("gpsimd", *free[("wd", gslot)])
                    guw[st_i] = dma(P, "gpsimd", semG, ctrG, gslot, sl(gu, gslot, H), img[jt])
                t = gtile.v; gtile.inc()
                slot = t % 4
                P.wait("tensor", *guw[0])
                if ("psD", slot) in free:
                    P.wait("tensor", *free[("psD", slot)])
                for c in range(CH):
                    if jt == 0:
                        P.wait("tensor", sem_vs, apply_v[c])
                    st, sp = (c == 0), (c == CH - 1)
                    gs = 0 * 2 + jt % 2
                    if sp:
                        P.add("tensor", lambda te, c=c, s=slot, gs=gs:
                              te.matmul(sl(ps_ringD, s, TT),
                                        gu[:, gs * H + c * 128: gs * H + (c + 1) * 128],
                                        xln2[:, c * TT:(c + 1) * TT],
                                        start=False, stop=True).then_inc(sem_ms, 1))
                    else:
                        P.add("tensor", lambda te, c=c, s=slot, gs=gs, st=st:
                              te.matmul(sl(ps_ringD, s, TT),
                                        gu[:, gs * H + c * 128: gs * H + (c + 1) * 128],
                                        xln2[:, c * TT:(c + 1) * TT],
                                        start=st, stop=False))
                m_gate = MS.inc()
                free[("gu", 0 * 2 + jt % 2)] = (sem_ms, m_gate)
                t2 = gtile.v; gtile.inc()
                slot2 = t2 % 4
                P.wait("tensor", *guw[1])
                if ("psD", slot2) in free:
                    P.wait("tensor", *free[("psD", slot2)])
                for c in range(CH):
                    st, sp = (c == 0), (c == CH - 1)
                    gs = 1 * 2 + jt % 2
                    if sp:
                        P.add("tensor", lambda te, c=c, s=slot2, gs=gs:
                              te.matmul(sl(ps_ringD, s, TT),
                                        gu[:, gs * H + c * 128: gs * H + (c + 1) * 128],
                                        xln2[:, c * TT:(c + 1) * TT],
                                        start=False, stop=True).then_inc(sem_ms, 1))
                    else:
                        P.add("tensor", lambda te, c=c, s=slot2, gs=gs, st=st:
                              te.matmul(sl(ps_ringD, s, TT),
                                        gu[:, gs * H + c * 128: gs * H + (c + 1) * 128],
                                        xln2[:, c * TT:(c + 1) * TT],
                                        start=st, stop=False))
                m_up = MS.inc()
                last_up[tb] = m_up
                free[("gu", 1 * 2 + jt % 2)] = (sem_ms, m_up)
                P.wait("scalar", sem_ms, m_gate)
                if ("scr", jt % 2) in free:
                    P.wait("scalar", *free[("scr", jt % 2)])
                P.add("scalar", lambda s, sp=slot, jt=jt, i=jt % 2:
                      s.activation(sl(sc_r, i, TT), sl(ps_ringD, sp, TT), AF.Silu,
                                   bias=0.0, scale=b_sb["ag"][:, jt:jt + 1]).then_inc(sem_ss, 1))
                s_silu = SS.inc()
                free[("psD", slot)] = (sem_ss, s_silu)
                P.wait("vector", sem_ms, m_up)
                P.add("vector", lambda v, sp=slot2, jt=jt, i=jt % 2:
                      v.tensor_scalar(sl(sc_e, i, TT), sl(ps_ringD, sp, TT),
                                      b_sb["au"][:, jt:jt + 1], None, ALU.mult)
                      .then_inc(sem_vs, 1))
                v_u = VS.inc()
                P.wait("vector", sem_ss, s_silu)
                P.wait("vector", sem_vs, v_u)
                if tb == 1 and jt == 0:
                    P.wait("vector", sem_ms, last_dn[0])
                P.add("vector", lambda v, jt=jt, i=jt % 2:
                      v.tensor_tensor(h_sb[:, jt * TT:(jt + 1) * TT], sl(sc_r, i, TT),
                                      sl(sc_e, i, TT), ALU.mult).then_inc(sem_vs, 1))
                v_h = VS.inc()
                h_rdy[jt] = v_h
                free[("psD", slot2)] = (sem_vs, v_h)
                free[("scr", jt % 2)] = (sem_vs, v_h)
            P.wait("tensor", sem_vs, h_rdy[IT - 1])
            for o in range(CH):
                wdv = []
                for half in range(2):
                    if ("wd", half) in free:
                        P.wait("gpsimd", *free[("wd", half)])
                    if ("gu", half) in free:
                        P.wait("gpsimd", *free[("gu", half)])
                    wdv.append(dma(P, "gpsimd", semG, ctrG, half,
                                   wstream[:, half * 5504:(half + 1) * 5504],
                                   wimg["wd"][o][:, half * 5504:(half + 1) * 5504]))
                t = gtile.v; gtile.inc()
                slot = t % 4
                if ("psD", slot) in free:
                    P.wait("tensor", *free[("psD", slot)])
                for c in range(IT):
                    half = 0 if c < 43 else 1
                    if c == 0:
                        P.wait("tensor", *wdv[0])
                    if c == 43:
                        P.wait("tensor", *wdv[1])
                    cc = c - 43 * half
                    st, sp = (c == 0), (c == IT - 1)
                    if sp or c == 42:
                        P.add("tensor", lambda te, c=c, cc=cc, half=half, s=slot, sp=sp, st=st:
                              te.matmul(sl(ps_ringD, s, TT),
                                        wstream[:, half * 5504 + cc * 128: half * 5504 + (cc + 1) * 128],
                                        h_sb[:, c * TT:(c + 1) * TT],
                                        start=st, stop=sp).then_inc(sem_ms, 1))
                    else:
                        P.add("tensor", lambda te, c=c, cc=cc, half=half, s=slot, st=st:
                              te.matmul(sl(ps_ringD, s, TT),
                                        wstream[:, half * 5504 + cc * 128: half * 5504 + (cc + 1) * 128],
                                        h_sb[:, c * TT:(c + 1) * TT],
                                        start=st, stop=False))
                    if c == 42:
                        free[("wd", 0)] = (sem_ms, MS.inc())
                m_dn = MS.inc()
                last_dn[tb] = m_dn
                free[("wd", 1)] = (sem_ms, m_dn)
                P.wait("scalar", sem_ms, m_dn)
                if ("sct", o % 2) in free:
                    P.wait("scalar", *free[("sct", o % 2)])
                P.add("scalar", lambda s, sp=slot, o=o, i=o % 2:
                      s.activation(sl(sc_t, i, TT), sl(ps_ringD, sp, TT), AF.Identity,
                                   bias=0.0, scale=b_sb["ad"][:, o:o + 1]).then_inc(sem_ss, 1))
                s_dn = SS.inc()
                free[("psD", slot)] = (sem_ss, s_dn)
                xsl = xre_ctr.v % 4
                xre_ctr.inc()
                if ("xre", xsl) in free:
                    P.wait("sync", *free[("xre", xsl)])
                x_ld = dma(P, "sync", semX, ctrX, xsl, sl(xre, xsl, TT),
                           x2d[o * 128:(o + 1) * 128, tb * TT:(tb + 1) * TT])
                P.wait("vector", sem_ss, s_dn)
                P.wait("vector", *x_ld)
                if ("sco", o % 2) in free:
                    P.wait("vector", *free[("sco", o % 2)])
                P.add("vector", lambda v, i=o % 2, xs=xsl:
                      v.tensor_tensor(sl(sc_o, i, TT), sl(sc_t, i, TT), sl(xre, xs, TT), ALU.add)
                      .then_inc(sem_vs, 1))
                v_out = VS.inc()
                free[("sct", o % 2)] = (sem_vs, v_out)
                free[("xre", xsl)] = (sem_vs, v_out)
                P.wait("sync", sem_vs, v_out)
                o_out = dma(P, "sync", semO, ctrO, o % 2,
                            outT[o * 128:(o + 1) * 128, tb * TT:(tb + 1) * TT], sl(sc_o, o % 2, TT))
                free[("sco", o % 2)] = o_out
        P.add("sync", lambda sy, v0=ctrO[0].v, v1=ctrO[1].v:
              (sy.wait_ge(semO[0], v0), sy.wait_ge(semO[1], v1)))
        with nc.Block() as b4:
            P.emit(b4)

    return nc, _emit_dumps(nc, dumps, locals())


def _emit_dumps(nc, dumps, env):
    if not dumps:
        return []
    names = []
    with nc.semaphore("dmp") as dsem, nc.Block() as bd:
        def _(sy):
            n = 0
            for nm in dumps:
                src = env[nm]
                ap = src if isinstance(src, bass.AP) else src.ap() if hasattr(src, "ap") else src[:]
                shape = list(ap.tensor.shape) if hasattr(ap, "tensor") else list(src.shape)
                dt = ap.dtype
                out = nc.dram_tensor("dump_" + nm, shape, dt, kind="ExternalOutput").ap()
                sy.dma_start(out[:], ap).then_inc(dsem, 16)
                n += 16
                names.append("dump_" + nm)
            sy.wait_ge(dsem, n)
        bd.sync(_)
    return names


_NC_CACHE = {}


def _prep_inputs(inputs):
    x = np.asarray(inputs["L0_attn_hidden_states"], dtype=np.float32)
    xf = np.ascontiguousarray(x.reshape(2 * 4096, H))

    def img(w, nblk, wide):
        w = np.asarray(w, dtype=np.float32)
        return np.ascontiguousarray(
            w.reshape(nblk, 128, wide // 128, 128).transpose(0, 3, 2, 1).reshape(nblk, 128, wide))

    def cols(v, n):
        return np.ascontiguousarray(np.asarray(v, dtype=np.float32).reshape(n, 128).T)

    common = {
        "wq": img(inputs["L0_attn_Wq"], QH, H),
        "wk": img(inputs["L0_attn_Wk"], KVH, H),
        "wv": img(inputs["L0_attn_Wv"], KVH, H),
        "wo": img(inputs["L0_attn_Wo"], CH, H),
        "wg": img(inputs["L0_ffn_Wg"], IT, H),
        "wu": img(inputs["L0_ffn_Wu"], IT, H),
        "wd": img(inputs["L0_ffn_Wd"], CH, IT * 128),
        "bq": cols(inputs["L0_attn_bq"], QH),
        "bk": cols(inputs["L0_attn_bk"], KVH),
        "bv": cols(inputs["L0_attn_bv"], KVH),
        "bo": cols(inputs["L0_attn_bo"], CH),
        "ag": cols(inputs["L0_ffn_ag"], IT),
        "au": cols(inputs["L0_ffn_au"], IT),
        "ad": cols(inputs["L0_ffn_ad"], CH),
        "ident": np.eye(128, dtype=ml_dtypes.bfloat16),
    }
    in_maps = []
    for c in range(8):
        m = dict(common)
        m["xT"] = np.ascontiguousarray(xf[c * TOK:(c + 1) * TOK].T)
        in_maps.append(m)
    return in_maps


def kernel(**inputs):
    in_maps = _prep_inputs(inputs)
    if "nc" not in _NC_CACHE:
        _NC_CACHE["nc"] = build()[0]
    nc = _NC_CACHE["nc"]
    res = run_bass_kernel_spmd(nc, in_maps, core_ids=list(range(8)))
    parts = [res.results[c]["outT"].T for c in range(8)]
    out = np.concatenate(parts, axis=0).reshape(2, 4096, H)
    return np.ascontiguousarray(out, dtype=np.float32)
